# version 01 (final)
# baseline (speedup 1.0000x reference)
import math
import numpy as np
import ml_dtypes
import concourse.bass as bass
import concourse.mybir as mybir
from concourse.bass_utils import run_bass_kernel_spmd

F32 = mybir.dt.float32
BF16 = mybir.dt.bfloat16
I8 = mybir.dt.int8
AF = mybir.ActivationFunctionType
ALU = mybir.AluOpType
AX = mybir.AxisListType
BF = ml_dtypes.bfloat16

T = 2048
D = 2048
KT = 16
EPS = 1e-5
C0 = 1.0 / math.sqrt(2048.0)
PI = math.pi

ENGS = ("sync", "gpsimd", "tensor", "vector", "scalar")

def sb(nc, name, shape, dt):
    return nc.sbuf_tensor(name, shape, dt).__enter__()


def psalloc(nc, name, shape, dt):
    return nc.psum_tensor(name, shape, dt).__enter__()



class Prog:

    def __init__(self):
        self.steps = {e: [] for e in ENGS}
        self.val = {e: 0 for e in ENGS}
        self.chan_val = {}

    def step(self, eng, fn, deps=(), ch=None):
        deps = tuple(d for d in deps if d is not None)
        if ch is not None:
            ch = "ch_" + ch
            self.chan_val[ch] = self.chan_val.get(ch, 0) + 16
            self.steps[eng].append((fn, deps, 16, ch))
            return (ch, self.chan_val[ch])
        self.steps[eng].append((fn, deps, 1, None))
        self.val[eng] += 1
        return (eng, self.val[eng])

    def load(self, fn, ch, deps=()):
        return self.step("sync", fn, deps, ch=ch)

    def store(self, fn, ch, deps=()):
        return self.step("gpsimd", fn, deps, ch=ch)

    def mm(self, fn, deps=()):
        return self.step("tensor", fn, deps)

    def vec(self, fn, deps=()):
        return self.step("vector", fn, deps)

    def sca(self, fn, deps=()):
        return self.step("scalar", fn, deps)

    def barrier_store(self):
        self.steps["gpsimd"].append((("WAIT_ALL_CH",), (), 0, None))


def _emit(nc, p):
    sems = {}
    ctxs = []
    for eng in ENGS:
        h = nc.semaphore("S_" + eng).__enter__()
        sems[eng] = h
    gps_channels = set()
    for eng in ENGS:
        for st in p.steps[eng]:
            if len(st) == 4 and st[3] is not None and eng == "gpsimd":
                gps_channels.add(st[3])
    for ch in sorted(p.chan_val):
        sems[ch] = nc.semaphore(ch).__enter__()
    with nc.Block() as blk:
        for eng in ENGS:
            steps = p.steps[eng]
            if not steps:
                continue

            def mk(eng=eng, steps=steps):
                def body(e):
                    maxseen = {}
                    for st in steps:
                        fn, deps, inc, ch = st
                        if isinstance(fn, tuple):
                            if fn[0] == "WAIT_ALL_CH":
                                for c in sorted(gps_channels):
                                    e.wait_ge(sems[c], p.chan_val[c])
                                continue
                            if fn[0] == "WAIT_SELF":
                                e.wait_ge(sems[eng], fn[1])
                                continue
                        for (peng, pval) in deps:
                            if maxseen.get(peng, -1) < pval:
                                e.wait_ge(sems[peng], pval)
                                maxseen[peng] = pval
                        instr = fn(e)
                        if instr is not None:
                            instr.then_inc(sems[ch if ch else eng], inc)
                return body
            getattr(blk, eng)(mk())


def load_fmaj(p, dst_sb, src_d, rows, cols, ch, row0=0, col0=0, deps=()):
    nk = rows // 128

    def fn(e):
        src = src_d[row0:row0 + rows, col0:col0 + cols]
        src = src.rearrange("(k pp) t -> pp k t", pp=128)
        dst = dst_sb[:, 0:nk * cols].rearrange("pp (k t) -> pp k t", t=cols)
        return e.dma_start(dst, src)
    return p.load(fn, ch, deps=deps)


def emit_ln(p, x_d, xn_d, b):
    prev_store = None
    prev_t8 = None
    prev_t3 = None
    prev_t4b = None
    prev_t6 = None
    cons = {}
    for i in range(KT):
        buf = b["xa"] if i % 2 == 0 else b["xb"]
        ld = p.load(lambda e, i=i, buf=buf: e.dma_start(
            buf[:], x_d[i * 128:(i + 1) * 128, :]), "ln_%d" % (i % 2),
            deps=[cons.get(i - 2)])
        t1 = p.vec(lambda e, buf=buf: e.reduce_sum(
            b["s1"][:], buf[:], axis=AX.X, negate=True), deps=[ld, prev_t3])
        t2 = p.vec(lambda e: e.tensor_scalar(
            b["s1"][:], b["s1"][:], 1.0 / D, None, ALU.mult), deps=[t1])
        t3 = p.vec(lambda e, buf=buf: e.tensor_scalar(
            b["xc"][:], buf[:], b["s1"][:], None, ALU.add),
            deps=[t2, prev_t8])
        t4a = p.sca(lambda e: e.activation(b["sq"][:], b["xc"][:], AF.Square),
                    deps=[t3, prev_t4b])
        t4 = p.vec(lambda e: e.reduce_sum(
            b["s2"][:], b["sq"][:], axis=AX.X), deps=[t4a, prev_t6])
        t5 = p.vec(lambda e: e.tensor_scalar(
            b["s2"][:], b["s2"][:], 1.0 / D, EPS, ALU.mult, ALU.add),
            deps=[t4])
        t6 = p.sca(lambda e: e.activation(b["s3"][:], b["s2"][:], AF.Sqrt),
                   deps=[t5])
        t7 = p.vec(lambda e: e.reciprocal(b["s3"][:], b["s3"][:]), deps=[t6])
        t8 = p.vec(lambda e: e.tensor_scalar(
            b["xno"][:], b["xc"][:], b["s3"][:], None, ALU.mult),
            deps=[t7, prev_store])
        prev_store = p.store(lambda e, i=i: e.dma_start(
            xn_d[i * 128:(i + 1) * 128, :], b["xno"][:]), "ln_xno",
            deps=[t8])
        prev_t8 = t8
        prev_t3 = t3
        prev_t4b = t4
        prev_t6 = t6
        cons[i] = t3
    return prev_store


def ln_bufs(nc, pref=""):
    return {
        "xa": nc.sbuf_tensor(pref + "xa", [128, D], F32).__enter__(),
        "xb": nc.sbuf_tensor(pref + "xb", [128, D], F32).__enter__(),
        "xc": nc.sbuf_tensor(pref + "xc", [128, D], F32).__enter__(),
        "sq": nc.sbuf_tensor(pref + "sq", [128, D], F32).__enter__(),
        "s1": nc.sbuf_tensor(pref + "s1", [128, 1], F32).__enter__(),
        "s2": nc.sbuf_tensor(pref + "s2", [128, 1], F32).__enter__(),
        "s3": nc.sbuf_tensor(pref + "s3", [128, 1], F32).__enter__(),
        "xno": nc.sbuf_tensor(pref + "xno", [128, D], BF16).__enter__(),
    }


def build_prep():
    nc = bass.Bass("TRN2", target_bir_lowering=False)
    x_d = nc.dram_tensor("x", [T, D], F32, kind="ExternalInput")
    wg8_d = nc.dram_tensor("wg8", [8192, D], I8, kind="ExternalInput")
    wu8_d = nc.dram_tensor("wu8", [8192, D], I8, kind="ExternalInput")
    wd8_d = nc.dram_tensor("wd8", [8192, D], I8, kind="ExternalInput")
    woutB_d = nc.dram_tensor("woutB", [D, D], F32, kind="ExternalInput")
    woB_d = nc.dram_tensor("woB", [D, D], F32, kind="ExternalInput")
    w1B_d = nc.dram_tensor("w1B", [8192, D], F32, kind="ExternalInput")
    w3B_d = nc.dram_tensor("w3B", [8192, D], F32, kind="ExternalInput")
    w2T_d = nc.dram_tensor("w2T", [8192, D], F32, kind="ExternalInput")
    h128_d = nc.dram_tensor("h128", [128, 128], BF16, kind="ExternalInput")

    xn_d = nc.dram_tensor("xn", [T, D], BF16, kind="ExternalOutput")
    wgb_d = nc.dram_tensor("wgb", [8192, D], BF16, kind="ExternalOutput")
    wub_d = nc.dram_tensor("wub", [8192, D], BF16, kind="ExternalOutput")
    wdb_d = nc.dram_tensor("wdb", [8192, D], BF16, kind="ExternalOutput")
    woutb_d = nc.dram_tensor("woutb", [D, D], BF16, kind="ExternalOutput")
    wob_d = nc.dram_tensor("wob", [D, D], BF16, kind="ExternalOutput")
    w1b_d = nc.dram_tensor("w1b", [8192, D], BF16, kind="ExternalOutput")
    w3b_d = nc.dram_tensor("w3b", [8192, D], BF16, kind="ExternalOutput")
    dst_d = nc.dram_tensor("dst", [8192, D], BF16, kind="ExternalOutput")

    p = Prog()
    ca0 = sb(nc, "ca0", [128, 2048], I8)
    ca1 = sb(nc, "ca1", [128, 2048], I8)
    co0 = sb(nc, "co0", [128, 2048], BF16)
    co1 = sb(nc, "co1", [128, 2048], BF16)
    fa0 = sb(nc, "fa0", [128, 2048], F32)
    fa1 = sb(nc, "fa1", [128, 2048], F32)
    fo0 = sb(nc, "fo0", [128, 2048], BF16)
    fo1 = sb(nc, "fo1", [128, 2048], BF16)
    h128s = sb(nc, "h128s", [128, 128], BF16)
    w2a0 = sb(nc, "w2a0", [128, 2048], F32)
    w2a1 = sb(nc, "w2a1", [128, 2048], F32)
    w2b = sb(nc, "w2b", [128, 2048], BF16)
    dsto = sb(nc, "dsto", [128, 2048], BF16)
    ps0 = psalloc(nc, "ps0", [128, 512], F32)
    ps1 = psalloc(nc, "ps1", [128, 512], F32)
    if True:
        lb = ln_bufs(nc)
        emit_ln(p, x_d, xn_d, lb)

        i8_jobs = []
        for src, dst in ((wg8_d, wgb_d), (wu8_d, wub_d), (wd8_d, wdb_d)):
            for rt in range(64):
                i8_jobs.append((src, dst, rt * 128))
        pc, pst = {}, {}
        for j, (src, dst, r0) in enumerate(i8_jobs):
            ibuf = ca0 if j % 2 == 0 else ca1
            obuf = co0 if j % 2 == 0 else co1
            ld = p.load(lambda e, src=src, r0=r0, ibuf=ibuf: e.dma_start(
                ibuf[:], src[r0:r0 + 128, :]), "ca_%d" % (j % 2),
                deps=[pc.get(j - 2)])
            cv = p.vec(lambda e, ibuf=ibuf, obuf=obuf: e.tensor_copy(
                obuf[:], ibuf[:]), deps=[ld, pst.get(j - 2)])
            st = p.store(lambda e, dst=dst, r0=r0, obuf=obuf: e.dma_start(
                dst[r0:r0 + 128, :], obuf[:]), "co_%d" % (j % 2), deps=[cv])
            pc[j], pst[j] = cv, st

        f32_jobs = []
        for src, dst, nt in ((woutB_d, woutb_d, 16), (woB_d, wob_d, 16),
                             (w1B_d, w1b_d, 64), (w3B_d, w3b_d, 64)):
            for rt in range(nt):
                f32_jobs.append((src, dst, rt * 128))
        pcf, pstf = {}, {}
        for j, (src, dst, r0) in enumerate(f32_jobs):
            ibuf = fa0 if j % 2 == 0 else fa1
            obuf = fo0 if j % 2 == 0 else fo1
            ld = p.load(lambda e, src=src, r0=r0, ibuf=ibuf: e.dma_start(
                ibuf[:], src[r0:r0 + 128, :]), "fa_%d" % (j % 2),
                deps=[pcf.get(j - 2)])
            cv = p.sca(lambda e, ibuf=ibuf, obuf=obuf: e.copy(
                obuf[:], ibuf[:]), deps=[ld, pstf.get(j - 2)])
            st = p.store(lambda e, dst=dst, r0=r0, obuf=obuf: e.dma_start(
                dst[r0:r0 + 128, :], obuf[:]), "fo_%d" % (j % 2), deps=[cv])
            pcf[j], pstf[j] = cv, st

        ldh = p.load(lambda e: e.dma_start(h128s[:], h128_d[:]), "h128")
        lastmm = {}
        evtok = {}
        sttokA = {}
        cvtok = {}
        for j1 in range(64):
            ibuf = w2a0 if j1 % 2 == 0 else w2a1
            ld = p.load(lambda e, j1=j1, ibuf=ibuf: e.dma_start(
                ibuf[:], w2T_d[j1 * 128:(j1 + 1) * 128, :]),
                "w2a_%d" % (j1 % 2), deps=[cvtok.get(j1 - 2)])
            cv = p.vec(lambda e, ibuf=ibuf: e.tensor_copy(w2b[:], ibuf[:]),
                       deps=[ld, lastmm.get(j1 - 1)])
            cvtok[j1] = cv
            evs = []
            for c in range(4):
                psu = ps0 if c % 2 == 0 else ps1
                war = evtok.get((j1, c - 2), evtok.get((j1 - 1, c + 2)))
                mm = p.mm(lambda e, c=c, psu=psu: e.matmul(
                    psu[:], h128s[:], w2b[:, c * 512:(c + 1) * 512],
                    start=True, stop=True), deps=[cv, ldh, war])
                if c == 3:
                    lastmm[j1] = mm
                ev = p.sca(lambda e, c=c, psu=psu: e.copy(
                    dsto[:, c * 512:(c + 1) * 512], psu[:]),
                    deps=[mm, sttokA.get(j1 - 1)])
                evtok[(j1, c)] = ev
                evs.append(ev)
            sttokA[j1] = p.store(lambda e, j1=j1: e.dma_start(
                dst_d[j1 * 128:(j1 + 1) * 128, :], dsto[:]), "dsto",
                deps=evs)
        p.barrier_store()
        _emit(nc, p)
    return nc


def build_proj(n_otiles, out_specs, nbias):
    nc = bass.Bass("TRN2", target_bir_lowering=False)
    xnT_d = nc.dram_tensor("xnT", [D, T], BF16, kind="ExternalInput")
    w_d = nc.dram_tensor("wB", [n_otiles * 128, D], F32, kind="ExternalInput")
    b_d = nc.dram_tensor("bcol", [128, nbias], F32, kind="ExternalInput")
    rows_per = {}
    for name, r0 in out_specs:
        rows_per[name] = max(rows_per.get(name, 0), r0 + 128)
    outs = {name: nc.dram_tensor(name, [rows, T], BF16, kind="ExternalOutput")
            for name, rows in rows_per.items()}

    p = Prog()
    xnTs = sb(nc, "xnTs", [128, KT * T], BF16)
    wf0 = sb(nc, "wf0", [128, 2048], F32)
    wf1 = sb(nc, "wf1", [128, 2048], F32)
    wb0 = sb(nc, "wb0", [128, 2048], BF16)
    wb1 = sb(nc, "wb1", [128, 2048], BF16)
    bc = sb(nc, "bc", [128, nbias], F32)
    ob0 = sb(nc, "ob0", [128, 512], BF16)
    ob1 = sb(nc, "ob1", [128, 512], BF16)
    ps0 = psalloc(nc, "ps0", [128, 512], F32)
    ps1 = psalloc(nc, "ps1", [128, 512], F32)
    if True:
        ldx = load_fmaj(p, xnTs, xnT_d, D, T, "xnTs")
        ldb = p.load(lambda e: e.dma_start(bc[:], b_d[:]), "bc")
        wcast = {}
        evt = {}
        stt = {}
        lastmm_w = {}
        for j in range(n_otiles):
            wf = wf0 if j % 2 == 0 else wf1
            wb = wb0 if j % 2 == 0 else wb1
            ld = p.load(lambda e, j=j, wf=wf: e.dma_start(
                wf[:], w_d[j * 128:(j + 1) * 128, :]), "wf_%d" % (j % 2),
                deps=[wcast.get(j - 2)])
            cv = p.vec(lambda e, wf=wf, wb=wb: e.tensor_copy(wb[:], wf[:]),
                       deps=[ld, lastmm_w.get(j - 2)])
            wcast[j] = cv
            for tc in range(4):
                psu = ps0 if tc % 2 == 0 else ps1
                war = evt.get((j, tc - 2), evt.get((j - 1, tc + 2)))
                mms = []
                for kt in range(KT):
                    mm = p.mm(lambda e, kt=kt, tc=tc, psu=psu, wb=wb:
                              e.matmul(psu[:],
                                       wb[:, kt * 128:(kt + 1) * 128],
                                       xnTs[:, kt * T + tc * 512:
                                            kt * T + (tc + 1) * 512],
                                       start=(kt == 0), stop=(kt == KT - 1)),
                              deps=[cv, ldx, ldb, war] if kt == 0 else ())
                    mms.append(mm)
                if tc == 3:
                    lastmm_w[j] = mms[-1]
                ob = ob0 if tc % 2 == 0 else ob1
                ev = p.sca(lambda e, j=j, psu=psu, ob=ob: e.activation(
                    ob[:], psu[:], AF.Identity, bias=bc[:, j:j + 1]),
                    deps=[mms[-1], stt.get((j, tc - 2),
                                           stt.get((j - 1, tc + 2)))])
                evt[(j, tc)] = ev
                name, r0 = out_specs[j]
                dstd = outs[name]
                stt[(j, tc)] = p.store(
                    lambda e, dstd=dstd, r0=r0, tc=tc, ob=ob: e.dma_start(
                        dstd[r0:r0 + 128, tc * 512:(tc + 1) * 512], ob[:]),
                    "ob_%d" % (tc % 2), deps=[ev])
        p.barrier_store()
        _emit(nc, p)
    return nc


QB = 256
NQB = T // QB


def build_attn0():
    S_FULL = 8192
    nc = bass.Bass("TRN2", target_bir_lowering=False)
    qT_d = nc.dram_tensor("qT", [D, T], BF16, kind="ExternalInput")
    kT_d = nc.dram_tensor("kTb", [D, S_FULL], BF16, kind="ExternalInput")
    v_d = nc.dram_tensor("vb", [S_FULL, D], BF16, kind="ExternalInput")
    id_d = nc.dram_tensor("ident", [128, 128], BF16, kind="ExternalInput")
    ctx_d = nc.dram_tensor("ctx", [T, D], BF16, kind="ExternalOutput")

    p = Prog()
    qs = sb(nc, "qs", [128, KT * QB], BF16)
    k0 = sb(nc, "k0", [128, KT * 512], BF16)
    k1 = sb(nc, "k1", [128, KT * 512], BF16)
    S0 = sb(nc, "S0", [128, S_FULL], BF16)
    S1 = sb(nc, "S1", [128, S_FULL], BF16)
    Pb = sb(nc, "Pb", [128, S_FULL], BF16)
    PT = sb(nc, "PT", [128, 64 * QB], BF16)
    cmax = sb(nc, "cmax", [128, 32], F32)
    mrow = sb(nc, "mrow", [128, 2], F32)
    negb = sb(nc, "negb", [128, 2], F32)
    rsum = sb(nc, "rsum", [128, 2], F32)
    rinv = sb(nc, "rinv", [128, 2], F32)
    vs = sb(nc, "vs", [128, 64 * 512], BF16)
    ids = sb(nc, "ids", [128, 128], BF16)
    cob0 = sb(nc, "cob0", [128, 512], BF16)
    cob1 = sb(nc, "cob1", [128, 512], BF16)
    psS0 = psalloc(nc, "psS0", [128, 512], F32)
    psS1 = psalloc(nc, "psS1", [128, 512], F32)
    psT = psalloc(nc, "psT", [128, 512], BF16)
    psC0 = psalloc(nc, "psC0", [128, 512], F32)
    psC1 = psalloc(nc, "psC1", [128, 512], F32)
    if True:
        Sbufs = {0: S0, 1: S1}
        PTv = PT.rearrange("pp (k q) -> pp k q", q=QB)
        psTv = psT.rearrange("pp (i j) -> pp i j", j=128)
        ldi = p.load(lambda e: e.dma_start(ids[:], id_d[:]), "ids")
        blk_done = None
        prev_m1 = {}
        prev_pexp = {}
        prev_tp_last = None
        prev_ctx_mm_last = None
        prev_rv = {}
        prev_cev = {}
        prev_cst = {}
        for qb in range(NQB):
            ldq = load_fmaj(p, qs, qT_d, D, QB, "qs", 0, qb * QB,
                            deps=[blk_done])
            ev_last = {}
            mx_last = {}
            kbuf_cons = {}
            for kc in range(16):
                kbuf = k0 if kc % 2 == 0 else k1
                ldk = load_fmaj(p, kbuf, kT_d, D, 512, "k_%d" % (kc % 2),
                                0, kc * 512,
                                deps=[kbuf_cons.get(kc - 2),
                                      blk_done if kc < 2 else None])
                for qt in range(2):
                    i = kc * 2 + qt
                    psu = psS0 if i % 2 == 0 else psS1
                    war = ev_last.get(i - 2)
                    mms = []
                    for kt in range(KT):
                        mm = p.mm(
                            lambda e, kt=kt, qt=qt, psu=psu, kbuf=kbuf:
                            e.matmul(psu[:],
                                     qs[:, kt * QB + qt * 128:
                                        kt * QB + (qt + 1) * 128],
                                     kbuf[:, kt * 512:(kt + 1) * 512],
                                     start=(kt == 0), stop=(kt == KT - 1)),
                            deps=[ldk, ldq, war] if kt == 0 else ())
                        mms.append(mm)
                    mx = p.vec(lambda e, qt=qt, kc=kc, psu=psu: e.reduce_max(
                        cmax[:, qt * 16 + kc:qt * 16 + kc + 1], psu[:],
                        axis=AX.X), deps=[mms[-1], prev_m1.get(qt)])
                    mx_last[i] = mx
                    Sb = Sbufs[qt]
                    ev = p.sca(lambda e, kc=kc, psu=psu, Sb=Sb: e.copy(
                        Sb[:, kc * 512:(kc + 1) * 512], psu[:]),
                        deps=[mx, prev_pexp.get(qt)])
                    ev_last[i] = ev
                kbuf_cons[kc] = ev_last[kc * 2 + 1]
            tp_last = prev_tp_last
            rv_tok = {}
            for qt in range(2):
                Sb = Sbufs[qt]
                m1 = p.vec(lambda e, qt=qt: e.reduce_max(
                    mrow[:, qt:qt + 1], cmax[:, qt * 16:(qt + 1) * 16],
                    axis=AX.X), deps=[mx_last[30 + qt], mx_last[30 + (1 - qt)]])
                prev_m1[qt] = m1
                m2 = p.vec(lambda e, qt=qt: e.tensor_scalar(
                    negb[:, qt:qt + 1], mrow[:, qt:qt + 1], -C0, None,
                    ALU.mult), deps=[m1, prev_rv.get(qt)])
                pexp = p.sca(lambda e, qt=qt, Sb=Sb: e.activation(
                    Pb[:], Sb[:], AF.Exp, bias=negb[:, qt:qt + 1], scale=C0,
                    accum_out=rsum[:, qt:qt + 1]),
                    deps=[m2, ev_last[30 + qt], tp_last])
                prev_pexp[qt] = pexp
                rv = p.vec(lambda e, qt=qt: e.reciprocal(
                    rinv[:, qt:qt + 1], rsum[:, qt:qt + 1]),
                    deps=[pexp, prev_cev.get(qt)])
                rv_tok[qt] = rv
                prev_rv[qt] = rv
                for g in range(16):
                    tps = []
                    for i4 in range(4):
                        kt64 = g * 4 + i4
                        tp = p.mm(lambda e, kt64=kt64, i4=i4: e.transpose(
                            psTv[:, i4, :],
                            Pb[:, kt64 * 128:(kt64 + 1) * 128], ids[:]),
                            deps=[pexp, ldi, tp_last] if i4 == 0
                            else [pexp, ldi])
                        tps.append(tp)
                    tp_last = p.vec(
                        lambda e, g=g, qt=qt: e.tensor_copy(
                            PTv[:, g * 4:g * 4 + 4, qt * 128:(qt + 1) * 128],
                            psTv[:, :, :]),
                        deps=tps + [prev_ctx_mm_last])
            prev_tp_last = tp_last
            vs_cons = None
            for dc in range(4):
                ldv = p.load(lambda e, dc=dc: e.dma_start(
                    vs[:].rearrange("pp (k t) -> pp k t", t=512),
                    v_d[0:S_FULL, dc * 512:(dc + 1) * 512]
                    .rearrange("(k pp) t -> pp k t", pp=128)), "vs",
                    deps=[vs_cons])
                last_mm_dc = None
                for qt in range(2):
                    psu = psC0 if qt == 0 else psC1
                    war = prev_cev.get(qt)
                    mms = []
                    for kt in range(64):
                        mm = p.mm(
                            lambda e, kt=kt, qt=qt, psu=psu: e.matmul(
                                psu[:], PTv[:, kt, qt * 128:(qt + 1) * 128],
                                vs[:, kt * 512:(kt + 1) * 512],
                                start=(kt == 0), stop=(kt == 63)),
                            deps=[ldv, tp_last, war] if kt == 0 else ())
                        mms.append(mm)
                    last_mm_dc = mms[-1]
                    cob = cob0 if qt == 0 else cob1
                    cev = p.vec(lambda e, qt=qt, psu=psu, cob=cob:
                                e.tensor_scalar(cob[:], psu[:],
                                                rinv[:, qt:qt + 1], None,
                                                ALU.mult),
                                deps=[mms[-1], rv_tok[qt],
                                      prev_cst.get(qt)])
                    prev_cev[qt] = cev
                    r0 = qb * QB + qt * 128
                    prev_cst[qt] = p.store(
                        lambda e, r0=r0, dc=dc, cob=cob: e.dma_start(
                            ctx_d[r0:r0 + 128, dc * 512:(dc + 1) * 512],
                            cob[:]), "cob_%d" % qt, deps=[cev])
                vs_cons = last_mm_dc
                if dc == 3:
                    blk_done = last_mm_dc
                    prev_ctx_mm_last = last_mm_dc
        p.barrier_store()
        _emit(nc, p)
    return nc


def build_mlp_oproj(has_scales):
    nc = bass.Bass("TRN2", target_bir_lowering=False)
    xT_d = nc.dram_tensor("resT", [D, T], F32, kind="ExternalInput")
    ctxT_d = nc.dram_tensor("ctxT", [D, T], BF16, kind="ExternalInput")
    xnT_d = nc.dram_tensor("xnT", [D, T], BF16, kind="ExternalInput")
    wg_d = nc.dram_tensor("wgB", [8192, D], BF16, kind="ExternalInput")
    wu_d = nc.dram_tensor("wuB", [8192, D], BF16, kind="ExternalInput")
    wd_d = nc.dram_tensor("wdB", [8192, D], BF16, kind="ExternalInput")
    wo_d = nc.dram_tensor("woB", [D, D], BF16, kind="ExternalInput")
    bo_d = nc.dram_tensor("bocol", [128, 16], F32, kind="ExternalInput")
    if has_scales:
        sg_d = nc.dram_tensor("sgcol", [128, 64], F32, kind="ExternalInput")
        su_d = nc.dram_tensor("sucol", [128, 64], F32, kind="ExternalInput")
        sd_d = nc.dram_tensor("sdcol", [128, 16], F32, kind="ExternalInput")
    out_d = nc.dram_tensor("x1T", [D, T], F32, kind="ExternalOutput")

    p = Prog()
    xns = sb(nc, "xns", [128, KT * 512], BF16)
    cxs = sb(nc, "cxs", [128, KT * 512], BF16)
    wgb0 = sb(nc, "wgb0", [128, 2048], BF16)
    wgb1 = sb(nc, "wgb1", [128, 2048], BF16)
    wub0 = sb(nc, "wub0", [128, 2048], BF16)
    wub1 = sb(nc, "wub1", [128, 2048], BF16)
    wdb0 = sb(nc, "wdb0", [128, 1024], BF16)
    wdb1 = sb(nc, "wdb1", [128, 1024], BF16)
    wob0 = sb(nc, "wob0", [128, 2048], BF16)
    wob1 = sb(nc, "wob1", [128, 2048], BF16)
    fuse = sb(nc, "fuse", [128, 64 * 512], BF16)
    gf = sb(nc, "gf", [128, 512], F32)
    gsil = sb(nc, "gsil", [128, 512], F32)
    ub = sb(nc, "ub", [128, 512], BF16)
    ffn8 = sb(nc, "ffn8", [128, 8 * 512], F32)
    att = sb(nc, "att", [128, 512], F32)
    xt0 = sb(nc, "xt0", [128, 512], F32)
    res1 = sb(nc, "res1", [128, 512], F32)
    res2 = sb(nc, "res2", [128, 512], F32)
    bco = sb(nc, "bco", [128, 16], F32)
    sgc = sb(nc, "sgc", [128, 64], F32)
    suc = sb(nc, "suc", [128, 64], F32)
    sdc = sb(nc, "sdc", [128, 16], F32)
    pd0 = psalloc(nc, "pd0", [128, 512], F32)
    pd1 = psalloc(nc, "pd1", [128, 512], F32)
    pd2 = psalloc(nc, "pd2", [128, 512], F32)
    pd3 = psalloc(nc, "pd3", [128, 512], F32)
    pd4 = psalloc(nc, "pd4", [128, 512], F32)
    pd5 = psalloc(nc, "pd5", [128, 512], F32)
    pd6 = psalloc(nc, "pd6", [128, 512], F32)
    pd7 = psalloc(nc, "pd7", [128, 512], F32)
    if True:
        pd = [pd0, pd1, pd2, pd3, pd4, pd5, pd6, pd7]
        ldb = p.load(lambda e: e.dma_start(bco[:], bo_d[:]), "bco")
        consts = [ldb]
        if has_scales:
            consts.append(p.load(lambda e: e.dma_start(sgc[:], sg_d[:]), "sgc"))
            consts.append(p.load(lambda e: e.dma_start(suc[:], su_d[:]), "suc"))
            consts.append(p.load(lambda e: e.dma_start(sdc[:], sd_d[:]), "sdc"))
        pbank_free = {}
        x_cons = None
        c_cons = None
        down_prev = None
        wcons = {}
        rs = {}
        r1_by_o8 = {}
        for tc in range(4):
            ldxn = load_fmaj(p, xns, xnT_d, D, 512, "xns", 0, tc * 512,
                             deps=[x_cons])
            ldcx = load_fmaj(p, cxs, ctxT_d, D, 512, "cxs", 0, tc * 512,
                             deps=[c_cons])
            prev_fu = {}
            for it in range(64):
                wg = wgb0 if it % 2 == 0 else wgb1
                wu = wub0 if it % 2 == 0 else wub1
                ldg = p.load(lambda e, it=it, wg=wg: e.dma_start(
                    wg[:], wg_d[it * 128:(it + 1) * 128, :]),
                    "wg_%d" % (it % 2), deps=[wcons.get(("g", it - 2))])
                ldu = p.load(lambda e, it=it, wu=wu: e.dma_start(
                    wu[:], wu_d[it * 128:(it + 1) * 128, :]),
                    "wu_%d" % (it % 2), deps=[wcons.get(("u", it - 2))])
                mg = []
                for kt in range(KT):
                    mm = p.mm(lambda e, kt=kt, wg=wg: e.matmul(
                        pd[0][:], wg[:, kt * 128:(kt + 1) * 128],
                        xns[:, kt * 512:(kt + 1) * 512],
                        start=(kt == 0), stop=(kt == KT - 1)),
                        deps=[ldg, ldxn, pbank_free.get(0)]
                        if kt == 0 else ())
                    mg.append(mm)
                mu = []
                for kt in range(KT):
                    mm = p.mm(lambda e, kt=kt, wu=wu: e.matmul(
                        pd[1][:], wu[:, kt * 128:(kt + 1) * 128],
                        xns[:, kt * 512:(kt + 1) * 512],
                        start=(kt == 0), stop=(kt == KT - 1)),
                        deps=[ldu, pbank_free.get(1)] if kt == 0 else ())
                    mu.append(mm)
                wcons[("g", it)] = mg[-1]
                wcons[("u", it)] = mu[-1]
                if has_scales:
                    g1 = p.vec(lambda e, it=it: e.tensor_scalar(
                        gf[:], pd[0][:], sgc[:, it:it + 1], None, ALU.mult),
                        deps=[mg[-1], prev_fu.get(("sil", it - 1))] + consts)
                    sil = p.sca(lambda e: e.activation(
                        gsil[:], gf[:], AF.Silu),
                        deps=[g1, prev_fu.get(it - 1)])
                    uev = p.vec(lambda e, it=it: e.tensor_scalar(
                        ub[:], pd[1][:], suc[:, it:it + 1], None, ALU.mult),
                        deps=[mu[-1], prev_fu.get(it - 1)])
                    pbank_free[0] = g1
                else:
                    sil = p.sca(lambda e: e.activation(
                        gsil[:], pd[0][:], AF.Silu),
                        deps=[mg[-1], prev_fu.get(it - 1)])
                    uev = p.vec(lambda e: e.tensor_copy(ub[:], pd[1][:]),
                                deps=[mu[-1], prev_fu.get(it - 1)])
                    pbank_free[0] = sil
                pbank_free[1] = uev
                fu = p.vec(lambda e, it=it: e.tensor_mul(
                    fuse[:, it * 512:(it + 1) * 512], gsil[:], ub[:]),
                    deps=[sil, uev, down_prev])
                prev_fu[it] = fu
                prev_fu[("sil", it)] = sil
            fuse_done = prev_fu[63]
            for oh in range(2):
                dmms = {}
                for it in range(64):
                    wdb = wdb0 if it % 2 == 0 else wdb1
                    ldd = p.load(lambda e, it=it, oh=oh, wdb=wdb: e.dma_start(
                        wdb[:], wd_d[it * 128:(it + 1) * 128,
                                     oh * 1024:(oh + 1) * 1024]),
                        "wd_%d" % (it % 2), deps=[wcons.get(("d", it - 2))])
                    for o8 in range(8):
                        dep0 = [ldd, fuse_done]
                        if it == 0:
                            dep0.append(pbank_free.get(o8))
                        mm = p.mm(lambda e, it=it, o8=o8, wdb=wdb: e.matmul(
                            pd[o8][:], wdb[:, o8 * 128:(o8 + 1) * 128],
                            fuse[:, it * 512:(it + 1) * 512],
                            start=(it == 0), stop=(it == 63)),
                            deps=dep0 if (it == 0 or o8 == 0) else ())
                        dmms[o8] = mm
                    wcons[("d", it)] = dmms[7]
                down_prev = dmms[7]
                for o8 in range(8):
                    ot = oh * 8 + o8
                    if has_scales:
                        fev = p.vec(lambda e, o8=o8, ot=ot: e.tensor_scalar(
                            ffn8[:, o8 * 512:(o8 + 1) * 512], pd[o8][:],
                            sdc[:, ot:ot + 1], None, ALU.mult),
                            deps=[dmms[o8], r1_by_o8.get(o8)] + consts)
                    else:
                        fev = p.vec(lambda e, o8=o8: e.tensor_copy(
                            ffn8[:, o8 * 512:(o8 + 1) * 512], pd[o8][:]),
                            deps=[dmms[o8], r1_by_o8.get(o8)])
                    wob = wob0 if o8 % 2 == 0 else wob1
                    ldo = p.load(lambda e, ot=ot, wob=wob: e.dma_start(
                        wob[:], wo_d[ot * 128:(ot + 1) * 128, :]),
                        "wo_%d" % (ot % 2), deps=[wcons.get(("o", ot - 2))])
                    omms = []
                    for kt in range(KT):
                        mm = p.mm(lambda e, kt=kt, o8=o8, wob=wob: e.matmul(
                            pd[o8][:], wob[:, kt * 128:(kt + 1) * 128],
                            cxs[:, kt * 512:(kt + 1) * 512],
                            start=(kt == 0), stop=(kt == KT - 1)),
                            deps=[ldo, ldcx, fev] if kt == 0 else ())
                        omms.append(mm)
                    wcons[("o", ot)] = omms[-1]
                    aev = p.sca(lambda e, ot=ot, o8=o8: e.activation(
                        att[:], pd[o8][:], AF.Identity, bias=bco[:, ot:ot + 1]),
                        deps=[omms[-1], ldb, rs.get("r1")])
                    pbank_free[o8] = aev
                    ldxt = p.load(lambda e, ot=ot, tc=tc: e.dma_start(
                        xt0[:], xT_d[ot * 128:(ot + 1) * 128,
                                     tc * 512:(tc + 1) * 512]), "xt0",
                        deps=[rs.get("r2")])
                    r1 = p.vec(lambda e, o8=o8: e.tensor_add(
                        res1[:], att[:], ffn8[:, o8 * 512:(o8 + 1) * 512]),
                        deps=[aev, fev, rs.get("r2")])
                    rs["r1"] = r1
                    r1_by_o8[o8] = r1
                    r2 = p.vec(lambda e: e.tensor_add(
                        res2[:], res1[:], xt0[:]),
                        deps=[r1, ldxt, rs.get("st")])
                    rs["r2"] = r2
                    rs["st"] = p.store(lambda e, ot=ot, tc=tc: e.dma_start(
                        out_d[ot * 128:(ot + 1) * 128,
                              tc * 512:(tc + 1) * 512], res2[:]), "res",
                        deps=[r2])
            x_cons = wcons[("u", 63)]
            c_cons = wcons[("o", 15)]
        p.barrier_store()
        _emit(nc, p)
    return nc


def build_wb_ln1():
    nc = bass.Bass("TRN2", target_bir_lowering=False)
    x1_d = nc.dram_tensor("x1", [T, D], F32, kind="ExternalInput")
    dp_d = nc.dram_tensor("dp", [64, 128 * 2048], BF16, kind="ExternalInput")
    h64_d = nc.dram_tensor("h64", [64, 64], BF16, kind="ExternalInput")
    xn1_d = nc.dram_tensor("xn1", [T, D], BF16, kind="ExternalOutput")
    e_d = nc.dram_tensor("eout", [64, 128 * 2048], BF16,
                         kind="ExternalOutput")

    p = Prog()
    NCH = 32
    h64s = sb(nc, "h64s", [64, 64], BF16)
    dp0 = sb(nc, "dp0", [64, 8192], BF16)
    dp1 = sb(nc, "dp1", [64, 8192], BF16)
    eo0 = sb(nc, "eo0", [64, 8192], BF16)
    eo1 = sb(nc, "eo1", [64, 8192], BF16)
    pe0 = psalloc(nc, "pe0", [64, 512], F32)
    pe1 = psalloc(nc, "pe1", [64, 512], F32)
    if True:
        lb = ln_bufs(nc, "ln1")
        emit_ln(p, x1_d, xn1_d, lb)
        ldh = p.load(lambda e: e.dma_start(h64s[:], h64_d[:]), "h64")
        scale = 1.0 / math.sqrt(8192.0)
        dcons = {}
        stt = {}
        evt = {}
        for ch in range(NCH):
            dbuf = dp0 if ch % 2 == 0 else dp1
            ebuf = eo0 if ch % 2 == 0 else eo1
            ld = p.load(lambda e, ch=ch, dbuf=dbuf: e.dma_start(
                dbuf[:], dp_d[:, ch * 8192:(ch + 1) * 8192]),
                "dp_%d" % (ch % 2), deps=[dcons.get(ch - 2)])
            evs = []
            for c in range(16):
                psu = pe0 if c % 2 == 0 else pe1
                war = evt.get((ch, c - 2), evt.get((ch - 1, c + 14)))
                mm = p.mm(lambda e, c=c, psu=psu, dbuf=dbuf: e.matmul(
                    psu[:], h64s[:], dbuf[:, c * 512:(c + 1) * 512],
                    start=True, stop=True), deps=[ld, ldh, war])
                if c == 15:
                    dcons[ch] = mm
                ev = p.sca(lambda e, c=c, psu=psu, ebuf=ebuf: e.activation(
                    ebuf[:, c * 512:(c + 1) * 512], psu[:], AF.Copy,
                    scale=scale), deps=[mm, stt.get(ch - 2)])
                evt[(ch, c)] = ev
                evs.append(ev)
            stt[ch] = p.store(lambda e, ch=ch, ebuf=ebuf: e.dma_start(
                e_d[:, ch * 8192:(ch + 1) * 8192], ebuf[:]),
                "eo_%d" % (ch % 2), deps=evs)
        p.barrier_store()
        _emit(nc, p)
    return nc


def build_mix():
    nc = bass.Bass("TRN2", target_bir_lowering=False)
    qkv_d = nc.dram_tensor("qkv1T", [4096, T], BF16, kind="ExternalInput")
    cos_d = nc.dram_tensor("cosF", [128, T], F32, kind="ExternalInput")
    sin_d = nc.dram_tensor("sinF", [128, T], F32, kind="ExternalInput")
    rT_d = nc.dram_tensor("rT", [128, 64], F32, kind="ExternalInput")
    ones_d = nc.dram_tensor("ones", [128, 128], BF16, kind="ExternalInput")
    out_d = nc.dram_tensor("outT", [D, T], BF16, kind="ExternalOutput")

    p = Prog()
    qh0 = sb(nc, "qh0", [128, T], BF16)
    qh1 = sb(nc, "qh1", [128, T], BF16)
    qsw = sb(nc, "qsw", [128, T], BF16)
    fcs = sb(nc, "fcs", [64, 512], BF16)
    vsb = sb(nc, "vsb", [128, 8 * T], BF16)
    cosf = sb(nc, "cosf", [128, T], F32)
    cosb = sb(nc, "cosb", [128, T], BF16)
    sinb = sb(nc, "sinb", [128, T], BF16)
    rTf = sb(nc, "rTf", [128, 64], F32)
    rTb = sb(nc, "rTb", [128, 64], BF16)
    oneb = sb(nc, "oneb", [128, 128], BF16)
    t1b = sb(nc, "t1b", [128, T], BF16)
    t2b = sb(nc, "t2b", [128, T], BF16)
    qr = sb(nc, "qr", [128, T], BF16)
    sqb = sb(nc, "sqb", [128, T], BF16)
    nf = sb(nc, "nf", [128, 512], F32)
    nf2 = sb(nc, "nf2", [128, 512], F32)
    nrv = sb(nc, "nrv", [128, 512], F32)
    qhat = sb(nc, "qhat", [128, 512], BF16)
    wrA = sb(nc, "wrA", [64, 512], F32)
    wrB = sb(nc, "wrB", [64, 512], F32)
    wrC = sb(nc, "wrC", [64, 512], F32)
    wrI = sb(nc, "wrI", [64, 512], mybir.dt.int32)
    feat = sb(nc, "feat", [128, 24 * T], BF16)
    prodb = sb(nc, "prodb", [128, 512], BF16)
    cbf = sb(nc, "cbf", [128, 512], BF16)
    tmb = sb(nc, "tmb", [128, 512], BF16)
    acc0 = sb(nc, "acc0", [128, 512], BF16)
    acc1 = sb(nc, "acc1", [128, 512], BF16)
    psN = psalloc(nc, "psN", [128, 512], F32)
    psP = psalloc(nc, "psP", [64, 512], F32)
    psC = psalloc(nc, "psC", [128, 512], F32)
    if True:
        ldcos = p.load(lambda e: e.dma_start(cosf[:], cos_d[:]), "cosf")
        ccos = p.vec(lambda e: e.tensor_copy(cosb[:], cosf[:]), deps=[ldcos])
        ldsin = p.load(lambda e: e.dma_start(cosf[:], sin_d[:]), "cosf",
                       deps=[ccos])
        csin = p.vec(lambda e: e.tensor_copy(sinb[:], cosf[:]), deps=[ldsin])
        ldrt = p.load(lambda e: e.dma_start(rTf[:], rT_d[:]), "rTf")
        crt = p.vec(lambda e: e.tensor_copy(rTb[:], rTf[:]), deps=[ldrt])
        ldone = p.load(lambda e: e.dma_start(oneb[:], ones_d[:]), "oneb")
        ldv = p.load(lambda e: e.dma_start(
            vsb[:].rearrange("pp (k t) -> pp k t", t=T),
            qkv_d[24 * 128:32 * 128, :]
            .rearrange("(k pp) t -> pp k t", pp=128)), "vsb")

        feat_done = {}
        pv = {}
        for hh in range(24):
            qh = qh0 if hh % 2 == 0 else qh1
            ldq = p.load(lambda e, hh=hh, qh=qh: e.dma_start(
                qh[:], qkv_d[hh * 128:(hh + 1) * 128, :]),
                "qh_%d" % (hh % 2), deps=[pv.get(("qh_cons", hh - 2))])
            lsw1 = p.load(lambda e, hh=hh: e.dma_start(
                qsw[0:64, :], qkv_d[hh * 128 + 64:(hh + 1) * 128, :]),
                "qswA", deps=[pv.get("qsw_cons")])
            lsw2 = p.load(lambda e, hh=hh: e.dma_start(
                qsw[64:128, :], qkv_d[hh * 128:hh * 128 + 64, :]),
                "qswB", deps=[pv.get("qsw_cons")])
            r1 = p.vec(lambda e, qh=qh: e.tensor_mul(
                t1b[:], qh[:], cosb[:]),
                deps=[ldq, ccos, pv.get("t1_cons")])
            r2 = p.vec(lambda e: e.tensor_mul(
                t2b[0:64, :], qsw[0:64, :], sinb[0:64, :]),
                deps=[lsw1, lsw2, csin, pv.get("t2_cons")])
            r3 = p.vec(lambda e: e.tensor_mul(
                t2b[64:128, :], qsw[64:128, :], sinb[64:128, :]),
                deps=[lsw1, lsw2])
            r4 = p.vec(lambda e: e.tensor_sub(
                qr[0:64, :], t1b[0:64, :], t2b[0:64, :]),
                deps=[r1, r2, pv.get("qr_cons")])
            r5 = p.vec(lambda e: e.tensor_add(
                qr[64:128, :], t1b[64:128, :], t2b[64:128, :]),
                deps=[r1, r3, pv.get("qr_cons")])
            pv[("qh_cons", hh)] = r3
            pv["qsw_cons"] = r3
            pv["t1_cons"] = r5
            pv["t2_cons"] = r5
            sqv = p.sca(lambda e: e.activation(sqb[:], qr[:], AF.Square),
                        deps=[r4, r5, pv.get("sq_cons")])
            last_pmm = None
            for tc in range(4):
                nmm = p.mm(lambda e, tc=tc: e.matmul(
                    psN[:], oneb[:], sqb[:, tc * 512:(tc + 1) * 512],
                    start=True, stop=True),
                    deps=[sqv, ldone, pv.get("psN_cons")])
                n1 = p.sca(lambda e: e.activation(nf[:], psN[:], AF.Sqrt),
                           deps=[nmm, pv.get("nf_cons")])
                pv["psN_cons"] = n1
                n2 = p.vec(lambda e: e.tensor_single_scalar(
                    nf2[:], nf[:], 1e-6, ALU.max),
                    deps=[n1, pv.get("nf2_cons")])
                pv["nf_cons"] = n2
                n3 = p.vec(lambda e: e.reciprocal(nrv[:], nf2[:]),
                           deps=[n2, pv.get("nrv_cons")])
                pv["nf2_cons"] = n3
                qht = p.vec(lambda e, tc=tc: e.tensor_mul(
                    qhat[:], qr[:, tc * 512:(tc + 1) * 512], nrv[:]),
                    deps=[n3, pv.get("qhat_cons")])
                pv["nrv_cons"] = qht
                pmm = p.mm(lambda e: e.matmul(
                    psP[:], rTb[:], qhat[:], start=True, stop=True),
                    deps=[qht, crt, pv.get("psP_cons")])
                pv["qhat_cons"] = pmm
                last_pmm = pmm
                a1 = p.vec(lambda e: e.tensor_single_scalar(
                    wrA[:], psP[:], 1.0 / (2 * PI), ALU.mult),
                    deps=[pmm, pv.get("wrA_cons")])
                a2 = p.vec(lambda e: e.tensor_copy(wrI[:], wrA[:]),
                           deps=[a1, pv.get("wrI_cons")])
                a3 = p.vec(lambda e: e.tensor_copy(wrA[:], wrI[:]),
                           deps=[a2])
                a4 = p.vec(lambda e: e.scalar_tensor_tensor(
                    wrB[:], wrA[:], -2 * PI, psP[:], ALU.mult, ALU.add),
                    deps=[a3, pmm, pv.get("wrB_cons")])
                s1 = p.sca(lambda e, hh=hh, tc=tc: e.activation(
                    feat[0:64, hh * T + tc * 512:hh * T + (tc + 1) * 512],
                    wrB[:], AF.Sin), deps=[a4])
                b0 = p.vec(lambda e: e.tensor_single_scalar(
                    wrC[:], psP[:], PI / 2, ALU.add),
                    deps=[pmm, pv.get("wrC_cons")])
                pv["psP_cons"] = b0
                b1 = p.vec(lambda e: e.tensor_single_scalar(
                    wrA[:], wrC[:], 1.0 / (2 * PI), ALU.mult),
                    deps=[b0, a4])
                b2 = p.vec(lambda e: e.tensor_copy(wrI[:], wrA[:]),
                           deps=[b1, a3])
                b3 = p.vec(lambda e: e.tensor_copy(wrA[:], wrI[:]),
                           deps=[b2])
                b4 = p.vec(lambda e: e.scalar_tensor_tensor(
                    wrB[:], wrA[:], -2 * PI, wrC[:], ALU.mult, ALU.add),
                    deps=[b3, s1])
                pv["wrA_cons"] = b4
                pv["wrC_cons"] = b4
                pv["wrI_cons"] = b3
                s2 = p.sca(lambda e: e.activation(
                    fcs[:], wrB[:], AF.Sin),
                    deps=[b4, pv.get("fcs_cons")])
                pv["wrB_cons"] = s2
                fmv = p.store(lambda e, hh=hh, tc=tc: e.dma_start(
                    feat[64:128, hh * T + tc * 512:hh * T + (tc + 1) * 512],
                    fcs[:]), "fcs", deps=[s2])
                pv["fcs_cons"] = fmv
                feat_done[(hh, tc)] = fmv
            pv["sq_cons"] = last_pmm
            pv["qr_cons"] = last_pmm
        hm = {}
        for h in range(16):
            for tc in range(4):
                accs = []
                for h2 in range(8):
                    pr = p.vec(lambda e, h=h, h2=h2, tc=tc: e.tensor_mul(
                        prodb[:],
                        feat[:, h * T + tc * 512:h * T + (tc + 1) * 512],
                        feat[:, (16 + h2) * T + tc * 512:
                             (16 + h2) * T + (tc + 1) * 512]),
                        deps=[feat_done[(h, tc)], feat_done[(16 + h2, tc)],
                              hm.get("pr_cons")])
                    cm = p.mm(lambda e: e.matmul(
                        psC[:], oneb[:], prodb[:], start=True, stop=True),
                        deps=[pr, ldone, hm.get("psC_cons")])
                    hm["pr_cons"] = cm
                    cv2 = p.vec(lambda e: e.tensor_scalar(
                        cbf[:], psC[:], 2.0 / 128.0, None, ALU.mult),
                        deps=[cm, hm.get("cbf_cons")])
                    hm["psC_cons"] = cv2
                    tm = p.vec(lambda e, h2=h2, tc=tc: e.tensor_mul(
                        tmb[:], cbf[:],
                        vsb[:, h2 * T + tc * 512:h2 * T + (tc + 1) * 512]),
                        deps=[cv2, ldv, hm.get("tmb_cons")])
                    hm["cbf_cons"] = tm
                    if h2 == 0:
                        acc = p.vec(lambda e: e.tensor_copy(acc0[:], tmb[:]),
                                    deps=[tm, hm.get("acc_cons")])
                    elif h2 == 1:
                        acc = p.vec(lambda e: e.tensor_add(
                            acc1[:], acc0[:], tmb[:]),
                            deps=[tm, accs[-1], hm.get("acc_cons")])
                    else:
                        src = acc1 if h2 % 2 == 0 else acc0
                        dst = acc0 if h2 % 2 == 0 else acc1
                        acc = p.vec(lambda e, src=src, dst=dst: e.tensor_add(
                            dst[:], src[:], tmb[:]), deps=[tm, accs[-1]])
                    hm["tmb_cons"] = acc
                    accs.append(acc)
                st = p.store(lambda e, h=h, tc=tc: e.dma_start(
                    out_d[h * 128:(h + 1) * 128, tc * 512:(tc + 1) * 512],
                    acc1[:]), "acc", deps=[accs[-1]])
                hm["acc_cons"] = st
        p.barrier_store()
        _emit(nc, p)
    return nc


def blockify(w, jtiles):
    J = jtiles
    Kt = w.shape[1] // 128
    L = w.reshape(J, 128, Kt, 128).transpose(0, 3, 2, 1)
    return np.ascontiguousarray(L).reshape(J * 128, Kt * 128)


def blockify_in(w):
    Kt = w.shape[0] // 128
    J = w.shape[1] // 128
    L = w.reshape(Kt, 128, J, 128).transpose(2, 1, 0, 3)
    return np.ascontiguousarray(L).reshape(J * 128, Kt * 128)


def col_pack(v):
    J = v.shape[0] // 128
    return np.ascontiguousarray(v.reshape(J, 128).T)


def hadamard(n):
    h = np.array([[1.0]], dtype=np.float64)
    while h.shape[0] < n:
        h = np.block([[h, h], [h, -h]])
    return h


def _run(nc, in_maps):
    res = run_bass_kernel_spmd(nc, in_maps, core_ids=list(range(8)))
    return res.results


def kernel(**inputs):
    g = lambda k: np.asarray(inputs[k])
    x = g("L0_attn_hidden_states")
    xf = np.ascontiguousarray(x.reshape(16384, 2048))

    wqkv = g("L0_attn_w_qkv")
    bqkv = g("L0_attn_b_qkv")
    wout = g("L0_attn_w_out")
    bout = g("L0_attn_b_out")
    wg8 = g("L0_ffn_w_gate")
    wu8 = g("L0_ffn_w_up")
    wd8 = g("L0_ffn_w_down")
    sg = g("L0_ffn_s_gate")
    su = g("L0_ffn_s_up")
    sd = g("L0_ffn_s_down")
    wq1 = g("L1_attn_Wq")
    bq1 = g("L1_attn_bq")
    wk1 = g("L1_attn_Wk")
    bk1 = g("L1_attn_bk")
    wv1 = g("L1_attn_Wv")
    bv1 = g("L1_attn_bv")
    wo1 = g("L1_attn_Wo")
    bo1 = g("L1_attn_bo")
    rmat = g("L1_attn_rand_matrix")
    w1 = g("L1_ffn_W1")
    w2 = g("L1_ffn_W2")
    w3 = g("L1_ffn_W3")

    wqkvB = blockify(wqkv, 48)
    woutB = blockify(wout, 16)
    wg8B = blockify_in(wg8)
    wu8B = blockify_in(wu8)
    w1B = blockify(w1, 64)
    w3B = blockify(w3, 64)
    wqkv1B = np.ascontiguousarray(np.concatenate(
        [blockify(wq1, 16), blockify(wk1, 8), blockify(wv1, 8)], axis=0))
    wo1B = blockify(wo1, 16)
    w2T = np.ascontiguousarray(w2.T)

    bqkv_c = col_pack(bqkv)
    bout_c = col_pack(bout)
    b1_c = np.ascontiguousarray(np.concatenate(
        [col_pack(bq1), col_pack(bk1), col_pack(bv1)], axis=1))
    bo1_c = col_pack(bo1)
    sg_c = col_pack(sg)
    su_c = col_pack(su)
    sd_c = col_pack(sd)

    h128 = hadamard(128).astype(BF)
    h64 = hadamard(64).astype(BF)
    ident = np.eye(128).astype(BF)
    ones = np.ones((128, 128), dtype=np.float64).astype(BF)
    rT = np.ascontiguousarray(rmat.T)

    inv_freq = 1.0 / (10000.0 ** (np.arange(0, 128, 2, dtype=np.float64) / 128.0))
    cos_t, sin_t = [], []
    for c in range(8):
        s0 = (c % 4) * T
        pos = np.arange(s0, s0 + T, dtype=np.float64)
        fr = pos[:, None] * inv_freq[None, :]
        emb = np.concatenate([fr, fr], axis=1)
        cos_t.append(np.ascontiguousarray(np.cos(emb).T).astype(np.float32))
        sin_t.append(np.ascontiguousarray(np.sin(emb).T).astype(np.float32))

    shards = [np.ascontiguousarray(xf[c * T:(c + 1) * T]) for c in range(8)]

    ncA = build_prep()
    inA = [dict(x=shards[c], wg8=wg8B, wu8=wu8B, wd8=wd8, woutB=woutB,
                woB=wo1B, w1B=w1B, w3B=w3B, w2T=w2T, h128=h128)
           for c in range(8)]
    rA = _run(ncA, inA)
    xnT = [np.ascontiguousarray(rA[c]["xn"].T) for c in range(8)]

    ncB = build_proj(48, [("qT", (j % 16) * 128) if j < 16 else
                          (("kT", (j - 16) * 128) if j < 32 else
                           ("vT", (j - 32) * 128)) for j in range(48)], 48)
    inB = [dict(xnT=xnT[c], wB=wqkvB, bcol=bqkv_c) for c in range(8)]
    rB = _run(ncB, inB)

    kTb, vbm = {}, {}
    for b in range(2):
        ks = [rB[4 * b + i]["kT"] for i in range(4)]
        vs = [rB[4 * b + i]["vT"] for i in range(4)]
        kTb[b] = np.ascontiguousarray(np.concatenate(ks, axis=1))
        vbm[b] = np.ascontiguousarray(np.concatenate(vs, axis=1).T)
    ncC = build_attn0()
    inC = [dict(qT=rB[c]["qT"], kTb=kTb[c // 4], vb=vbm[c // 4], ident=ident)
           for c in range(8)]
    rC = _run(ncC, inC)

    xT = [np.ascontiguousarray(shards[c].T) for c in range(8)]
    ctxT = [np.ascontiguousarray(rC[c]["ctx"].T) for c in range(8)]
    ncD = build_mlp_oproj(True)
    inD = [dict(resT=xT[c], ctxT=ctxT[c], xnT=xnT[c], wgB=rA[c]["wgb"],
                wuB=rA[c]["wub"], wdB=rA[c]["wdb"], woB=rA[c]["woutb"],
                bocol=bout_c, sgcol=sg_c, sucol=su_c, sdcol=sd_c)
           for c in range(8)]
    rD = _run(ncD, inD)

    x1 = [np.ascontiguousarray(rD[c]["x1T"].T) for c in range(8)]
    dp = [np.ascontiguousarray(rA[c]["dst"]).reshape(64, 128 * 2048)
          for c in range(8)]
    ncE = build_wb_ln1()
    inE = [dict(x1=x1[c], dp=dp[c], h64=h64) for c in range(8)]
    rE = _run(ncE, inE)

    xn1T = [np.ascontiguousarray(rE[c]["xn1"].T) for c in range(8)]
    ncF1 = build_proj(32, [("qkv1T", j * 128) for j in range(32)], 32)
    inF1 = [dict(xnT=xn1T[c], wB=wqkv1B, bcol=b1_c) for c in range(8)]
    rF1 = _run(ncF1, inF1)

    ncF2 = build_mix()
    inF2 = [dict(qkv1T=rF1[c]["qkv1T"], cosF=cos_t[c], sinF=sin_t[c],
                 rT=rT, ones=ones) for c in range(8)]
    rF2 = _run(ncF2, inF2)

    scrT = {}
    for b in range(2):
        OT = np.concatenate([rF2[4 * b + i]["outT"] for i in range(4)],
                            axis=1)
        arr = OT.reshape(16, 128, 8192).transpose(0, 2, 1)
        scr = np.ascontiguousarray(arr).reshape(8192, 2048)
        scrT[b] = np.ascontiguousarray(scr.T)

    w2p = [np.ascontiguousarray(rE[c]["eout"]).reshape(8192, 2048)
           for c in range(8)]
    ncG = build_mlp_oproj(False)
    inG = []
    for c in range(8):
        s0 = (c % 4) * T
        inG.append(dict(
            resT=rD[c]["x1T"],
            ctxT=np.ascontiguousarray(scrT[c // 4][:, s0:s0 + T]),
            xnT=xn1T[c], wgB=rA[c]["w1b"], wuB=rA[c]["w3b"], wdB=w2p[c],
            woB=rA[c]["wob"], bocol=bo1_c))
    rG = _run(ncG, inG)

    out = np.concatenate([rG[c]["x1T"].T for c in range(8)], axis=0)
    return np.ascontiguousarray(out).reshape(2, 8192, 2048).astype(np.float32)
